# version 02 (final)
# baseline (speedup 1.0000x reference)
import numpy as np
import ml_dtypes
import concourse.bass as bass
import concourse.mybir as mybir
from concourse.bass_utils import run_bass_kernel_spmd

F32 = mybir.dt.float32
BF16 = mybir.dt.bfloat16
AF = mybir.ActivationFunctionType
ALU = mybir.AluOpType
AX = mybir.AxisListType
BF = ml_dtypes.bfloat16

ENG = ("tensor", "vector", "scalar", "sync", "gpsimd")
ENGS = ENG + ("gdma",)

T = 2048
D = 1024
NCORES = 8


class Prog:

    def __init__(self, nc):
        self.nc = nc
        self.q = {e: [] for e in ENG}
        self.val = {e: 0 for e in ENGS}
        self.wr = {}
        self.rd = {}
        self.waited = {e: {e2: 0 for e2 in ENGS} for e in ENG}
        self.sems = {}
        self.groups = {}
        self._order = []
        self.block = None
        self._cms = []

    def sbuf(self, name, shape, dtype, group="main"):
        side = "left" if group == "main" else "right"
        cm = self.nc.sbuf_tensor(name, shape, dtype, side=side)
        h = cm.__enter__()
        self.groups.setdefault(group, []).append(cm)
        self._order.append(cm)
        return h

    def psum(self, name, shape, dtype, group="main"):
        cm = self.nc.psum_tensor(name, shape, dtype)
        h = cm.__enter__()
        self.groups.setdefault(group, []).append(cm)
        self._order.append(cm)
        return h

    def free_group(self, group):
        self.barrier()
        self.flush()
        for cm in reversed(self.groups.pop(group, [])):
            cm.__exit__(None, None, None)
            self._order.remove(cm)

    def emit(self, eng, build, reads=(), writes=(), dma=False):
        sem_eng = "gdma" if (dma and eng == "gpsimd") else eng
        deps = {}

        def add(e2, v):
            if deps.get(e2, 0) < v:
                deps[e2] = v

        for k in reads:
            w = self.wr.get(k)
            if w:
                add(*w)
        for k in writes:
            w = self.wr.get(k)
            if w:
                add(*w)
            for e2, v in self.rd.get(k, {}).items():
                add(e2, v)
        if dma:
            if deps.get(sem_eng, 0) < self.val[sem_eng]:
                deps[sem_eng] = self.val[sem_eng]
        waits = []
        for e2 in ENGS:
            v = deps.get(e2, 0)
            if v <= 0:
                continue
            if e2 == eng == "tensor":
                continue
            if self.waited[eng][e2] < v:
                waits.append((e2, v))
                self.waited[eng][e2] = v
        inc = 16 if dma else 1
        self.val[sem_eng] += inc
        after = self.val[sem_eng]
        sems = self.sems
        mysem = sem_eng

        def run(h, waits=tuple(waits), build=build, inc=inc):
            for e2, v in waits:
                h.wait_ge(sems[e2], v)
            ins = build(h)
            ins.then_inc(sems[mysem], inc)

        self.q[eng].append(run)
        for k in reads:
            d = self.rd.setdefault(k, {})
            if d.get(sem_eng, 0) < after:
                d[sem_eng] = after
        for k in writes:
            self.wr[k] = (sem_eng, after)
            self.rd[k] = {}
        return after

    def barrier(self):
        vals = dict(self.val)
        for e in ENG:
            others = tuple((e2, v) for e2, v in vals.items()
                           if e2 != e and v > self.waited[e][e2])
            if not others:
                continue
            sems = self.sems

            def run(h, others=others):
                for e2, v in others:
                    h.wait_ge(sems[e2], v)

            self.q[e].append(run)
            for e2, v in others:
                self.waited[e][e2] = v

    def start(self):
        for e in ENGS:
            cm = self.nc.semaphore(f"sem_{e}")
            self.sems[e] = cm.__enter__()
            self._cms.append(cm)
        cm = self.nc.Block()
        self.block = cm.__enter__()
        self._cms.append(cm)

    def flush(self):
        for e in ENG:
            if not self.q[e]:
                continue
            lst = self.q[e]
            self.q[e] = []

            def body(h, lst=lst):
                for f in lst:
                    f(h)

            getattr(self.block, e)(body)

    def finish(self):
        self.barrier()
        self.flush()
        for cm in reversed(self._cms):
            cm.__exit__(None, None, None)
        self.groups.clear()
        for cm2 in reversed(self._order):
            cm2.__exit__(None, None, None)
        self._order = []



def mm(p, out_ap, lhsT_ap, rhs_ap, reads, wkey, start=True, stop=True):
    p.emit("tensor",
           lambda h: h.matmul(out_ap, lhsT_ap, rhs_ap, start=start, stop=stop),
           reads=reads, writes=(wkey,))


def transpose(p, out_ap, in_ap, ident_ap, reads, wkey):
    p.emit("tensor", lambda h: h.transpose(out_ap, in_ap, ident_ap),
           reads=reads, writes=(wkey,))


def vtt(p, out_ap, a_ap, b_ap, op, reads, wkey, eng="vector"):
    p.emit(eng, lambda h: h.tensor_tensor(out_ap, a_ap, b_ap, op),
           reads=reads, writes=(wkey,))


def vts(p, out_ap, in_ap, s1, op, reads, wkey, eng="vector", s2=None, op1=None):
    if op1 is None:
        p.emit(eng, lambda h: h.tensor_scalar(out_ap, in_ap, s1, None, op0=op),
               reads=reads, writes=(wkey,))
    else:
        p.emit(eng, lambda h: h.tensor_scalar(out_ap, in_ap, s1, s2, op0=op, op1=op1),
               reads=reads, writes=(wkey,))


def act(p, out_ap, in_ap, func, reads, wkey, bias=None, scale=None, eng="scalar"):
    kw = {}
    if bias is not None:
        kw["bias"] = bias
    if scale is not None:
        kw["scale"] = scale
    p.emit(eng, lambda h: h.activation(out_ap, in_ap, func, **kw),
           reads=reads, writes=(wkey,))


def memset(p, ap, val, wkey, eng="vector"):
    p.emit(eng, lambda h: h.memset(ap, val), reads=(), writes=(wkey,))


def copy(p, out_ap, in_ap, reads, wkey, eng="scalar"):
    if eng == "scalar":
        p.emit(eng, lambda h: h.copy(out_ap, in_ap), reads=reads, writes=(wkey,))
    else:
        p.emit(eng, lambda h: h.tensor_copy(out_ap, in_ap), reads=reads, writes=(wkey,))


def rsum(p, out_ap, in_ap, reads, wkey, eng="vector"):
    p.emit(eng, lambda h: h.reduce_sum(out_ap, in_ap, axis=AX.X),
           reads=reads, writes=(wkey,))


def rmax(p, out_ap, in_ap, reads, wkey, eng="vector"):
    p.emit(eng, lambda h: h.reduce_max(out_ap, in_ap, axis=AX.X),
           reads=reads, writes=(wkey,))


def recip(p, out_ap, in_ap, reads, wkey):
    p.emit("vector", lambda h: h.reciprocal(out_ap, in_ap),
           reads=reads, writes=(wkey,))


def dma(p, out_ap, in_ap, reads, wkey, eng="sync"):
    p.emit(eng, lambda h: h.dma_start(out_ap, in_ap),
           reads=reads, writes=(wkey,), dma=True)


def emit_rms(p, out_ap, x_ap, sc, st, xkey, okey, key, ebias, ekey, W=1024):
    act(p, sc[:, 0:W], x_ap, AF.Square, (xkey,), key + "_sq")
    rsum(p, st[:, 0:1], sc[:, 0:W], (key + "_sq",), key + "_s")
    act(p, st[:, 1:2], st[:, 0:1], AF.Sqrt, (key + "_s", ekey), key + "_s",
        bias=ebias)
    recip(p, st[:, 2:3], st[:, 1:2], (key + "_s",), key + "_s")
    vts(p, st[:, 2:3], st[:, 2:3], float(np.sqrt(W)), ALU.mult,
        (key + "_s",), key + "_s")
    vts(p, out_ap, x_ap, st[:, 2:3], ALU.mult, (xkey, key + "_s"), okey)


def emit_ln_full(p, out_ap, x_ap, xc, sq, st, xkey, okey, key, ebias, ekey,
                 W=1024):
    rsum(p, st[:, 0:1], x_ap, (xkey,), key + "_s")
    vts(p, st[:, 0:1], st[:, 0:1], -1.0 / W, ALU.mult, (key + "_s",), key + "_s")
    vts(p, xc[:, 0:W], x_ap, st[:, 0:1], ALU.add, (xkey, key + "_s"), key + "_xc")
    act(p, sq[:, 0:W], xc[:, 0:W], AF.Square, (key + "_xc",), key + "_sq")
    rsum(p, st[:, 1:2], sq[:, 0:W], (key + "_sq",), key + "_v")
    act(p, st[:, 2:3], st[:, 1:2], AF.Sqrt, (key + "_v", ekey), key + "_v",
        bias=ebias)
    recip(p, st[:, 3:4], st[:, 2:3], (key + "_v",), key + "_v")
    vts(p, st[:, 3:4], st[:, 3:4], float(np.sqrt(W)), ALU.mult, (key + "_v",), key + "_v")
    vts(p, out_ap, xc[:, 0:W], st[:, 3:4], ALU.mult, (key + "_xc", key + "_v"), okey)



def build_consts():
    c = {}
    c["ones_f"] = np.ones((128, 128), np.float32)
    c["ones_b"] = np.ones((128, 128), BF)
    c["ident_f"] = np.eye(128, dtype=np.float32)
    c["ident_b"] = np.eye(128, dtype=np.float32).astype(BF)
    P = np.zeros((128, 128), np.float32)
    for i in range(64):
        P[2 * i + 1, 2 * i] = 1.0
        P[2 * i, 2 * i + 1] = 1.0
    c["rope_P"] = P
    t = np.arange(T, dtype=np.float64)
    i = np.arange(32, dtype=np.float64)
    inv = 10000.0 ** (-2.0 * i / 64.0)
    fr = np.outer(inv, t)
    cosr = np.cos(fr)
    sinr = np.sin(fr)
    cos128 = np.zeros((128, T))
    sin128 = np.zeros((128, T))
    for p in range(128):
        ii = (p % 64) // 2
        cos128[p] = cosr[ii]
        sin128[p] = -sinr[ii] if (p % 2 == 0) else sinr[ii]
    qs = 1.0 / 8.0
    c["cos_q"] = (cos128 * qs).astype(np.float32)
    c["sin_q"] = (sin128 * qs).astype(np.float32)
    c["cos_k"] = cos128.astype(np.float32)
    c["sin_k"] = sin128.astype(np.float32)
    caus = np.zeros((128, 4 * 512), np.float32)
    for v in range(4):
        k = np.arange(128)[:, None]
        q = np.arange(512)[None, :]
        caus[:, v * 512:(v + 1) * 512] = (128 * v + k <= q)
    c["caus"] = caus.astype(BF)
    L = np.zeros((128, 512), np.float32)
    for r in range(2):
        cp = np.arange(128)[:, None]
        cc = np.arange(256)[None, :]
        L[:, r * 256:(r + 1) * 256] = (128 * r + cp <= cc)
    c["L256_b"] = L.astype(BF)
    c["L256_f"] = L.astype(np.float32)
    c["rcnt"] = np.broadcast_to(1.0 / (np.arange(256, dtype=np.float32) + 1.0),
                                (128, 256)).copy()
    sm = np.zeros((128, 16 * 8), np.float32)
    for tt in range(16):
        for n in range(8):
            if not (n < tt // 2):
                sm[:, 8 * tt + n] = -1e30
    c["scmask"] = sm
    return c


CONST_CACHE = None


def get_consts():
    global CONST_CACHE
    if CONST_CACHE is None:
        CONST_CACHE = build_consts()
    return CONST_CACHE


def declare_consts(nc, names):
    cs = get_consts()
    aps = {}
    for n in names:
        a = cs[n]
        dt = BF16 if a.dtype == BF else F32
        aps[n] = nc.dram_tensor("c_" + n, list(a.shape), dt, kind="ExternalInput").ap()
    return aps


def load_consts(p, aps, group="consts"):
    cs = get_consts()
    sb = {}
    for n, ap in aps.items():
        a = cs[n]
        dt = BF16 if a.dtype == BF else F32
        t = p.sbuf("s_" + n, list(a.shape), dt, group=group)
        dma(p, t[:, :], ap[:, :], (), "c_" + n)
        sb[n] = t
    return sb


def const_inmaps(names):
    cs = get_consts()
    return {"c_" + n: np.ascontiguousarray(cs[n]) for n in names}



def _load_weight_tiles(p, dst_bf, src_ap, rows, cols, wstg, key, nrt=128,
                       cast_eng="vector"):
    nt = rows // nrt
    for r in range(nt):
        dma(p, wstg[0:nrt, 0:cols], src_ap[r * nrt:(r + 1) * nrt, 0:cols],
            ("wstg",), "wstg")
        copy(p, dst_bf[0:nrt, r * cols:(r + 1) * cols], wstg[0:nrt, 0:cols],
             ("wstg",), key, eng=cast_eng)


def build_stageA(nc, upto=4, ln_tiles=16):
    x_d = nc.dram_tensor("x", [T, D], F32, kind="ExternalInput").ap()
    wqa_d = nc.dram_tensor("Wqa", [D, 64], F32, kind="ExternalInput").ap()
    wqb_d = nc.dram_tensor("Wqb", [64, D], F32, kind="ExternalInput").ap()
    wkva_d = nc.dram_tensor("Wkva", [D, 32], F32, kind="ExternalInput").ap()
    wkvb_d = nc.dram_tensor("Wkvb", [32, 512], F32, kind="ExternalInput").ap()
    wout_d = nc.dram_tensor("Wout", [D, D], F32, kind="ExternalInput").ap()
    bout_d = nc.dram_tensor("bout", [1, D], F32, kind="ExternalInput").ap()
    y_d = nc.dram_tensor("y", [T, D], F32, kind="ExternalOutput").ap()
    cnames = ["ones_f", "ones_b", "ident_b", "rope_P",
              "cos_q", "sin_q", "cos_k", "sin_k", "caus"]
    caps = declare_consts(nc, cnames)

    p = Prog(nc)
    p.start()
    cs = load_consts(p, {n: caps[n] for n in
                         ["ones_f", "ones_b", "ident_b", "caus"]}, group="main")

    pb = [p.psum(f"pb{i}", [128, 512], F32) for i in range(6)]
    ptb = [p.psum(f"ptb{i}", [128, 128], BF16) for i in range(2)]

    wqaB = p.sbuf("wqaB", [128, 512], BF16)
    wqbB = p.sbuf("wqbB", [64, 1024], BF16)
    wkvaB = p.sbuf("wkvaB", [128, 256], BF16)
    wkvbB = p.sbuf("wkvbB", [32, 512], BF16)
    woutB = p.sbuf("woutB", [128, 8192], BF16)
    boutB = p.sbuf("boutB", [128, 1024], F32)
    xnT = p.sbuf("xnT", [128, 16384], BF16)
    qropeT = p.sbuf("qropeT", [128, 16384], BF16)
    kropeT = p.sbuf("kropeT", [128, 4096], BF16)
    vtok = p.sbuf("vtok", [128, 4096], BF16)
    attnT = p.sbuf("attnT", [128, 16384], BF16)
    eb = p.sbuf("eb", [128, 2], F32)
    memset(p, eb[:, 0:1], 1e-5 * D, "eb")

    wstg = p.sbuf("wstg", [128, 1024], F32, group="prep")
    for r in range(8):
        dma(p, wstg[:, r * 64:(r + 1) * 64], wqa_d[r * 128:(r + 1) * 128, :],
            ("wstg",), "wstg")
    copy(p, wqaB[:, :], wstg[:, 0:512], ("wstg",), "wqaB", eng="vector")
    dma(p, wstg[0:64, 0:1024], wqb_d[:, :], ("wstg",), "wstg")
    copy(p, wqbB[:, :], wstg[0:64, 0:1024], ("wstg",), "wqbB", eng="scalar")
    for r in range(8):
        dma(p, wstg[:, r * 32:(r + 1) * 32],
            wkva_d[r * 128:(r + 1) * 128, :], ("wstg",), "wstg")
    copy(p, wkvaB[:, :], wstg[:, 0:256], ("wstg",), "wkvaB", eng="vector")
    dma(p, wstg[0:32, 0:512], wkvb_d[:, :], ("wstg",), "wstg")
    copy(p, wkvbB[:, :], wstg[0:32, 0:512], ("wstg",), "wkvbB", eng="scalar")
    _load_weight_tiles(p, woutB, wout_d, D, D, wstg, "woutB")
    dma(p, wstg[0:1, 0:1024], bout_d[:, :], ("wstg",), "wstg")
    for dc in range(2):
        mm(p, pb[5][:, 0:512], cs["ones_f"][0:1, :],
           wstg[0:1, dc * 512:(dc + 1) * 512], ("wstg", "c_ones_f"), "pb5")
        copy(p, boutB[:, dc * 512:(dc + 1) * 512], pb[5][:, 0:512],
             ("pb5",), "boutB", eng="scalar")
    if upto == 0:
        dma(p, y_d[0:128, :], boutB[:, :], ("boutB",), "yprobe", eng="gpsimd")
        p.finish()
        return nc
    p.free_group("prep")

    g = "ln"
    xt = [p.sbuf(f"xt{i}", [128, 1024], F32, group=g) for i in range(2)]
    xnbf = [p.sbuf(f"xnbf{i}", [128, 1024], BF16, group=g) for i in range(2)]
    xc = p.sbuf("xc", [128, 1024], F32, group=g)
    sq = p.sbuf("sq", [128, 1024], F32, group=g)
    st = p.sbuf("st", [128, 8], F32, group=g)
    for tt in range(ln_tiles):
        b = tt % 2
        dma(p, xt[b][:, :], x_d[tt * 128:(tt + 1) * 128, :], (f"xt{b}",), f"xt{b}")
        emit_ln_full(p, xnbf[b][:, :], xt[b][:, :], xc, sq, st,
                     f"xt{b}", f"xnbf{b}", "lnA", eb[:, 0:1], "eb", W=D)
        for dt in range(8):
            pk = f"ptb{dt % 2}"
            sl = ptb[dt % 2][:, :]
            transpose(p, sl, xnbf[b][:, dt * 128:(dt + 1) * 128],
                      cs["ident_b"][:, :], (f"xnbf{b}", "c_ident_b"), pk)
            copy(p, xnT[:, dt * 2048 + tt * 128: dt * 2048 + (tt + 1) * 128],
                 sl, (pk,), "xnT", eng="scalar" if dt % 2 else "vector")
    if upto == 1:
        pr = p.sbuf("probe", [128, 1024], F32)
        copy(p, pr[:, :], xnT[:, 0:1024], ("xnT",), "probe", eng="vector")
        dma(p, y_d[0:128, :], pr[:, :], ("probe",), "yprobe", eng="gpsimd")
        p.finish()
        return nc
    p.free_group(g)

    g = "rope"
    rp = load_consts(p, {n: caps[n] for n in
                         ["rope_P", "cos_q", "sin_q", "cos_k", "sin_k"]}, group=g)
    qlatB = p.sbuf("qlatB", [64, 2048], BF16, group=g)
    kvlatB = p.sbuf("kvlatB", [32, 2048], BF16, group=g)
    qf = p.sbuf("qf", [128, 2048], F32, group=g)
    r1 = p.sbuf("r1", [128, 512], F32, group=g)
    r2 = p.sbuf("r2", [128, 512], F32, group=g)
    vbf = p.sbuf("vbf", [128, 512], BF16, group=g)
    for qc in range(4):
        for dt in range(8):
            mm(p, pb[0][0:64, 0:512], wqaB[:, dt * 64:(dt + 1) * 64],
               xnT[:, dt * 2048 + qc * 512: dt * 2048 + (qc + 1) * 512],
               ("wqaB", "xnT"), "pb0", start=(dt == 0), stop=(dt == 7))
        copy(p, qlatB[0:64, qc * 512:(qc + 1) * 512], pb[0][0:64, 0:512],
             ("pb0",), "qlatB", eng="scalar")
        for dt in range(8):
            mm(p, pb[1][0:32, 0:512], wkvaB[:, dt * 32:(dt + 1) * 32],
               xnT[:, dt * 2048 + qc * 512: dt * 2048 + (qc + 1) * 512],
               ("wkvaB", "xnT"), "pb1", start=(dt == 0), stop=(dt == 7))
        copy(p, kvlatB[0:32, qc * 512:(qc + 1) * 512], pb[1][0:32, 0:512],
             ("pb1",), "kvlatB", eng="scalar")

    def rope_tile(lhsT_ap, lkey, lat, latkey, outT, out_dt, okey, cosn, sinn,
                  qremap=False):
        for qc in range(4):
            mm(p, pb[2][:, 0:512], lhsT_ap,
               lat[:, qc * 512:(qc + 1) * 512], (lkey, latkey), "pb2")
            copy(p, qf[:, qc * 512:(qc + 1) * 512], pb[2][:, 0:512],
                 ("pb2",), f"qf{qc}", eng="scalar")
        for qc in range(4):
            mm(p, pb[3][:, 0:512], rp["rope_P"][:, :],
               qf[:, qc * 512:(qc + 1) * 512], ("c_rope_P", f"qf{qc}"), "pb3")
            vtt(p, r1[:, :], qf[:, qc * 512:(qc + 1) * 512],
                rp[cosn][:, qc * 512:(qc + 1) * 512], ALU.mult,
                (f"qf{qc}", "c_" + cosn), "r1")
            vtt(p, r2[:, :], pb[3][:, 0:512],
                rp[sinn][:, qc * 512:(qc + 1) * 512], ALU.mult,
                ("pb3", "c_" + sinn), "r2")
            if not qremap:
                vtt(p, outT[:, out_dt * 2048 + qc * 512: out_dt * 2048 + (qc + 1) * 512],
                    r1[:, :], r2[:, :], ALU.add, ("r1", "r2"), okey)
            else:
                for rh in range(2):
                    hb = 2 * out_dt + rh
                    qp = (hb // 4) % 2
                    qt = (hb % 4) + 4 * (hb // 8)
                    vtt(p, outT[qp * 64:(qp + 1) * 64,
                                qt * 2048 + qc * 512: qt * 2048 + (qc + 1) * 512],
                        r1[rh * 64:(rh + 1) * 64, :], r2[rh * 64:(rh + 1) * 64, :],
                        ALU.add, ("r1", "r2"), okey)

    for dt in range(8):
        rope_tile(wqbB[0:64, dt * 128:(dt + 1) * 128], "wqbB",
                  qlatB[0:64, :], "qlatB", qropeT, dt, "qropeT", "cos_q", "sin_q",
                  qremap=True)
    for dt2 in range(2):
        rope_tile(wkvbB[0:32, dt2 * 128:(dt2 + 1) * 128], "wkvbB",
                  kvlatB[0:32, :], "kvlatB", kropeT, dt2, "kropeT", "cos_k", "sin_k")
    for dt2 in range(2):
        for qc in range(4):
            mm(p, pb[2][:, 0:512], wkvbB[0:32, 256 + dt2 * 128: 256 + (dt2 + 1) * 128],
               kvlatB[0:32, qc * 512:(qc + 1) * 512], ("wkvbB", "kvlatB"), "pb2")
            copy(p, vbf[:, :], pb[2][:, 0:512], ("pb2",), "vbf", eng="scalar")
            for j in range(4):
                ti = qc * 4 + j
                pk = f"ptb{j % 2}"
                sl = ptb[j % 2][:, :]
                transpose(p, sl, vbf[:, j * 128:(j + 1) * 128],
                          cs["ident_b"][:, :], ("vbf", "c_ident_b"), pk)
                copy(p, vtok[:, dt2 * 2048 + ti * 128: dt2 * 2048 + (ti + 1) * 128],
                     sl, (pk,), "vtok", eng="vector")
    if upto == 2:
        pr = p.sbuf("probe", [128, 1024], F32)
        copy(p, pr[:, :], qropeT[:, 0:1024], ("qropeT",), "probe", eng="vector")
        dma(p, y_d[0:128, :], pr[:, :], ("probe",), "yprobe", eng="gpsimd")
        p.finish()
        return nc
    p.free_group(g)

    g = "attn"
    expT = p.sbuf("expT", [128, 8192], BF16, group=g)
    recipS = p.sbuf("recipS", [1, 512], F32, group=g)
    recB = p.sbuf("recB", [64, 512], F32, group=g)
    for h in range(16):
        kvh = h // 4
        qtile, qbase = h // 2, (h % 2) * 64
        for qc in range(4):
            nkt = 4 * (qc + 1)
            kp = (kvh % 2) * 64
            qt16 = ((h % 4) + 4 * (h // 8)) * 2048
            for kt in range(nkt):
                bk = f"pb{kt % 2}"
                mm(p, pb[kt % 2][:, 0:512],
                   kropeT[kp:kp + 64,
                          (kvh // 2) * 2048 + kt * 128: (kvh // 2) * 2048 + (kt + 1) * 128],
                   qropeT[kp:kp + 64,
                          qt16 + qc * 512: qt16 + (qc + 1) * 512],
                   ("kropeT", "qropeT"), bk)
                act(p, expT[:, kt * 512:(kt + 1) * 512], pb[kt % 2][:, 0:512],
                    AF.Exp, (bk,), f"expT{kt}")
                v = kt - 4 * qc
                if v >= 0:
                    vtt(p, expT[:, kt * 512:(kt + 1) * 512],
                        expT[:, kt * 512:(kt + 1) * 512],
                        cs["caus"][:, v * 512:(v + 1) * 512], ALU.mult,
                        (f"expT{kt}", "c_caus"), f"expT{kt}")
            for kt in range(nkt):
                mm(p, pb[2][0:1, 0:512], cs["ones_b"][:, 0:1],
                   expT[:, kt * 512:(kt + 1) * 512],
                   ("c_ones_b", f"expT{kt}"), "pb2",
                   start=(kt == 0), stop=(kt == nkt - 1))
            recip(p, recipS[0:1, :], pb[2][0:1, 0:512], ("pb2",), "recipS")
            mm(p, pb[3][0:64, 0:512], cs["ones_f"][0:1, 0:64],
               recipS[0:1, 0:512], ("c_ones_f", "recipS"), "pb3")
            copy(p, recB[:, :], pb[3][0:64, 0:512], ("pb3",), "recB", eng="scalar")
            for kt in range(nkt):
                mm(p, pb[4][0:64, 0:512],
                   vtok[:, (kvh // 2) * 2048 + kt * 128 + (kvh % 2) * 64:
                        (kvh // 2) * 2048 + kt * 128 + (kvh % 2) * 64 + 64],
                   expT[:, kt * 512:(kt + 1) * 512],
                   ("vtok", f"expT{kt}"), "pb4",
                   start=(kt == 0), stop=(kt == nkt - 1))
            vtt(p, attnT[qbase:qbase + 64,
                         qtile * 2048 + qc * 512: qtile * 2048 + (qc + 1) * 512],
                pb[4][0:64, 0:512], recB[:, :], ALU.mult,
                ("pb4", "recB"), "attnT")
    if upto == 3:
        pr = p.sbuf("probe", [128, 1024], F32, group=g)
        copy(p, pr[:, :], attnT[:, 0:1024], ("attnT",), "probe", eng="vector")
        dma(p, y_d[0:128, :], pr[:, :], ("probe",), "yprobe", eng="gpsimd")
        p.finish()
        return nc
    x1t = [p.sbuf(f"x1t{i}", [128, 1024], F32, group=g) for i in range(2)]
    o1 = [p.sbuf(f"o1{i}", [128, 512], F32, group=g) for i in range(2)]
    for tt in range(16):
        b = tt % 2
        dma(p, x1t[b][:, :], x_d[tt * 128:(tt + 1) * 128, :],
            (f"x1t{b}",), f"x1t{b}")
        for dc in range(2):
            bk = f"pb{4 + dc}"
            for dt in range(8):
                mm(p, pb[4 + dc][:, 0:512],
                   attnT[:, dt * 2048 + tt * 128: dt * 2048 + (tt + 1) * 128],
                   woutB[:, dt * 1024 + dc * 512: dt * 1024 + (dc + 1) * 512],
                   ("attnT", "woutB"), bk, start=(dt == 0), stop=(dt == 7))
            vtt(p, o1[dc][:, :], pb[4 + dc][:, 0:512],
                boutB[:, dc * 512:(dc + 1) * 512], ALU.add,
                (bk, "boutB"), f"o1{dc}")
            vtt(p, x1t[b][:, dc * 512:(dc + 1) * 512],
                x1t[b][:, dc * 512:(dc + 1) * 512], o1[dc][:, :], ALU.add,
                (f"x1t{b}", f"o1{dc}"), f"x1t{b}")
        dma(p, y_d[tt * 128:(tt + 1) * 128, :], x1t[b][:, :],
            (f"x1t{b}",), f"y{b}", eng="gpsimd")
    p.finish()
    return nc


def stageA_inmaps(x_core, w):
    cnames = ["ones_f", "ones_b", "ident_b", "rope_P",
              "cos_q", "sin_q", "cos_k", "sin_k", "caus"]
    m = const_inmaps(cnames)
    kvb = w["L0_attn_Wkvb"]
    perm = np.empty(512, np.int64)
    for h in range(4):
        for dd in range(64):
            perm[h * 64 + dd] = h * 128 + 2 * dd
            perm[256 + h * 64 + dd] = h * 128 + 2 * dd + 1
    m.update({
        "x": np.ascontiguousarray(x_core),
        "Wqa": w["L0_attn_Wqa"],
        "Wqb": w["L0_attn_Wqb"],
        "Wkva": w["L0_attn_Wkva"],
        "Wkvb": np.ascontiguousarray(kvb[:, perm]),
        "Wout": w["L0_attn_Wout"],
        "bout": np.ascontiguousarray(w["L0_attn_bout"][None, :]),
    })
    return m



def build_stageB(nc, layer=0, upto=9):
    x_d = nc.dram_tensor("x", [T, D], F32, kind="ExternalInput").ap()
    w1_d = nc.dram_tensor("W1", [8, D, 4096], F32, kind="ExternalInput").ap()
    b1_d = nc.dram_tensor("b1", [8, 4096], F32, kind="ExternalInput").ap()
    w2_d = nc.dram_tensor("W2", [8, 4096, D], F32, kind="ExternalInput").ap()
    b2_d = nc.dram_tensor("b2", [8, D], F32, kind="ExternalInput").ap()
    gw_d = nc.dram_tensor("gateW", [D, 8], F32, kind="ExternalInput").ap()
    if layer == 0:
        shw_d = nc.dram_tensor("sharedW", [D, D], F32, kind="ExternalInput").ap()
        shb_d = nc.dram_tensor("sharedb", [1, D], F32, kind="ExternalInput").ap()
        gam_d = nc.dram_tensor("gamma", [1, D], F32, kind="ExternalInput").ap()
        bet_d = nc.dram_tensor("beta", [1, D], F32, kind="ExternalInput").ap()
    else:
        gb_d = nc.dram_tensor("gateb", [1, 8], F32, kind="ExternalInput").ap()
    y_d = nc.dram_tensor("y", [T, D], F32, kind="ExternalOutput").ap()
    cnames = ["ones_f", "ident_b", "ident_f"]
    caps = declare_consts(nc, cnames)

    p = Prog(nc)
    p.start()
    cs = load_consts(p, {n: caps[n] for n in cnames}, group="main")

    pbH = [p.psum(f"pbH{i}", [128, 512], F32) for i in range(2)]
    pbO = [p.psum(f"pbO{i}", [128, 512], F32) for i in range(4)]
    pbG = pbH[0]
    ptf = p.psum("ptf", [128, 128], F32)
    ptb = p.psum("ptbb", [128, 128], BF16)

    AFU = AF.Gelu if layer == 0 else AF.Relu

    xnT = p.sbuf("xnT", [128, 8192], BF16)
    OUT = p.sbuf("OUT", [128, 8192], F32)
    w1bf = p.sbuf("w1bf", [128, 32768], BF16)
    hbf = p.sbuf("hbf", [128, 16384], BF16)
    gateB = p.sbuf("gateB", [128, 64], BF16)
    gateLo = p.sbuf("gateLo", [128, 64], BF16)
    comb = p.sbuf("comb", [128, 64], F32)
    combT = p.sbuf("combT", [8, 1024], BF16)
    b2bf = p.sbuf("b2bf", [8, 1024], BF16)
    b1col = p.sbuf("b1col", [128, 32], F32)
    eb = p.sbuf("eb", [128, 2], F32)
    memset(p, eb[:, 0:1], 1e-5 * D, "eb")
    if layer == 0:
        shbB = p.sbuf("shbB", [128, 1024], F32)
        gamB = p.sbuf("gamB", [128, 1024], F32)
        betB = p.sbuf("betB", [128, 1024], F32)
    else:
        gbB = p.sbuf("gbB", [128, 8], F32)

    wstg = p.sbuf("wstg", [128, 1024], F32, group="prep")
    for r in range(8):
        dma(p, wstg[:, r * 8:(r + 1) * 8], gw_d[r * 128:(r + 1) * 128, :],
            ("wstg",), "wstg")
    copy(p, gateB[:, :], wstg[:, 0:64], ("wstg",), "gateB", eng="vector")
    copy(p, wstg[:, 64:128], gateB[:, :], ("gateB", "wstg"), "ghif", eng="vector")
    vtt(p, wstg[:, 128:192], wstg[:, 0:64], wstg[:, 64:128], ALU.subtract,
        ("wstg", "ghif"), "glo")
    copy(p, gateLo[:, :], wstg[:, 128:192], ("glo",), "gateLo", eng="vector")
    dma(p, wstg[0:8, 0:1024], b2_d[:, :], ("wstg", "glo", "ghif"), "wstg")
    copy(p, b2bf[:, :], wstg[0:8, 0:1024], ("wstg",), "b2bf", eng="scalar")
    if layer == 0:
        for nm, dd, dst in (("shb", shb_d, shbB), ("gam", gam_d, gamB),
                            ("bet", bet_d, betB)):
            dma(p, wstg[0:1, 0:1024], dd[:, :], ("wstg",), "wstg")
            for dc in range(2):
                mm(p, pbG[:, 0:512], cs["ones_f"][0:1, :],
                   wstg[0:1, dc * 512:(dc + 1) * 512], ("wstg", "c_ones_f"), "pbH0")
                copy(p, dst[:, dc * 512:(dc + 1) * 512], pbG[:, 0:512],
                     ("pbH0",), nm, eng="scalar")
    else:
        dma(p, wstg[0:1, 0:8], gb_d[:, :], ("wstg",), "wstg")
        mm(p, pbG[:, 0:8], cs["ones_f"][0:1, :], wstg[0:1, 0:8],
           ("wstg", "c_ones_f"), "pbH0")
        copy(p, gbB[:, :], pbG[:, 0:8], ("pbH0",), "gbB", eng="scalar")
    if upto == 0:
        dma(p, y_d[0:128, :], (gamB if layer == 0 else shbB_dummy)[:, :], ("gam",), "yp", eng="gpsimd")
        p.finish()
        return nc
    p.free_group("prep")

    g = "ln"
    xt0 = p.sbuf("xt0", [128, 1024], F32, group=g)
    xnf = p.sbuf("xnf", [128, 1024], F32, group=g)
    xnbf = p.sbuf("xnbf", [128, 1024], BF16, group=g)
    xlobf = p.sbuf("xlobf", [128, 1024], BF16, group=g)
    xloT = p.sbuf("xloT", [128, 1024], BF16, group=g)
    combbf = p.sbuf("combbf", [128, 8], BF16, group=g)
    xc = p.sbuf("xc", [128, 1024], F32, group=g)
    sq = p.sbuf("sq", [128, 1024], F32, group=g)
    st = p.sbuf("st", [128, 16], F32, group=g)
    fnrm = p.sbuf("fnrm", [128, 1024], F32, group=g)
    w1stg0 = p.sbuf("w1stg0", [128, 1024], F32, group=g)
    w1stg = [w1stg0, w1stg0]
    w2stg = [p.sbuf(f"w2stg{i}", [128, 512], F32, group=g) for i in range(2)]
    w2bf = [p.sbuf(f"w2bf{i}", [128, 512], BF16, group=g) for i in range(2)]
    b1stg = p.sbuf("b1stg", [32, 128], F32, group=g)
    tmpS = [p.sbuf(f"tmpS{i}", [128, 512], F32, group=g) for i in range(2)]
    xres0 = p.sbuf("xres0", [128, 1024], F32, group=g)
    xres = [xres0, xres0]
    lgb = p.sbuf("lgb", [128, 64], F32, group=g)

    for hf in range(2):
        lg = lgb
        for t2 in range(8):
            tt = hf * 8 + t2
            dma(p, xt0[:, :], x_d[tt * 128:(tt + 1) * 128, :], ("xt0",), "xt0")
            emit_ln_full(p, xnf[:, :], xt0[:, :], xc, sq, st,
                         "xt0", "xnf", "lnB", eb[:, 0:1], "eb", W=D)
            copy(p, xnbf[:, :], xnf[:, :], ("xnf",), "xnbf", eng="scalar")
            vtt(p, xlobf[:, :], xnf[:, :], xnbf[:, :], ALU.subtract,
                ("xnf", "xnbf"), "xlobf")
            for dt in range(8):
                transpose(p, ptb[:, :], xnbf[:, dt * 128:(dt + 1) * 128],
                          cs["ident_b"][:, :], ("xnbf", "c_ident_b"), "ptb")
                copy(p, xnT[:, dt * 1024 + t2 * 128: dt * 1024 + (t2 + 1) * 128],
                     ptb[:, :], ("ptb",), "xnT", eng="scalar")
                transpose(p, ptb[:, :], xlobf[:, dt * 128:(dt + 1) * 128],
                          cs["ident_b"][:, :], ("xlobf", "c_ident_b", "xnT"), "ptb")
                copy(p, xloT[:, dt * 128:(dt + 1) * 128], ptb[:, :], ("ptb",),
                     "xloT", eng="scalar")
            for dt in range(8):
                mm(p, pbG[:, 0:8],
                   xnT[:, dt * 1024 + t2 * 128: dt * 1024 + (t2 + 1) * 128],
                   gateB[:, dt * 8:(dt + 1) * 8], ("xnT", "gateB"), "pbH0",
                   start=(dt == 0), stop=False)
            for dt in range(8):
                mm(p, pbG[:, 0:8], xloT[:, dt * 128:(dt + 1) * 128],
                   gateB[:, dt * 8:(dt + 1) * 8], ("xloT", "gateB"), "pbH0",
                   start=False, stop=False)
            for dt in range(8):
                mm(p, pbG[:, 0:8],
                   xnT[:, dt * 1024 + t2 * 128: dt * 1024 + (t2 + 1) * 128],
                   gateLo[:, dt * 8:(dt + 1) * 8], ("xnT", "gateLo"), "pbH0",
                   start=False, stop=(dt == 7))
            if layer == 0:
                copy(p, lg[:, 0:8], pbG[:, 0:8], ("pbH0",), "lg", eng="scalar")
            else:
                vtt(p, lg[:, 0:8], pbG[:, 0:8], gbB[:, :], ALU.add,
                    ("pbH0", "gbB"), "lg")
            rmax(p, st[:, 0:1], lg[:, 0:8], ("lg",), "gm")
            vts(p, st[:, 1:2], st[:, 0:1], -1.0, ALU.mult, ("gm",), "gm")
            act(p, lg[:, 8:16], lg[:, 0:8], AF.Exp, ("lg", "gm"), "pe",
                bias=st[:, 1:2])
            rsum(p, st[:, 2:3], lg[:, 8:16], ("pe",), "gs")
            recip(p, st[:, 3:4], st[:, 2:3], ("gs",), "gs")
            vts(p, lg[:, 16:24], lg[:, 8:16], st[:, 3:4], ALU.mult, ("pe", "gs"), "pr")
            rmax(p, st[:, 4:5], lg[:, 16:24], ("pr",), "m1")
            vts(p, lg[:, 24:32], lg[:, 16:24], st[:, 4:5], ALU.is_ge,
                ("pr", "m1"), "mk1")
            vts(p, lg[:, 32:40], lg[:, 24:32], 1e30, ALU.mult, ("mk1",), "mk1")
            vtt(p, lg[:, 40:48], lg[:, 16:24], lg[:, 32:40], ALU.subtract,
                ("pr", "mk1"), "prm")
            rmax(p, st[:, 5:6], lg[:, 40:48], ("prm",), "m2")
            if layer == 0:
                vts(p, lg[:, 48:56], lg[:, 16:24], st[:, 5:6], ALU.is_ge,
                    ("pr", "m2"), "sel")
                vtt(p, st[:, 6:7], st[:, 4:5], st[:, 5:6], ALU.add, ("m1", "m2"), "m12")
                recip(p, st[:, 7:8], st[:, 6:7], ("m12",), "m12")
                vtt(p, lg[:, 56:64], lg[:, 16:24], lg[:, 48:56], ALU.mult,
                    ("pr", "sel"), "w12")
                vts(p, comb[:, t2 * 8:(t2 + 1) * 8], lg[:, 56:64], st[:, 7:8],
                    ALU.mult, ("w12", "m12"), "comb")
            else:
                vts(p, comb[:, t2 * 8:(t2 + 1) * 8], lg[:, 16:24], st[:, 5:6],
                    ALU.is_ge, ("pr", "m2"), "comb")
            copy(p, combbf[:, :], comb[:, t2 * 8:(t2 + 1) * 8], ("comb",),
                 "combbf", eng="vector")
            transpose(p, ptb[0:8, :], combbf[:, :],
                      cs["ident_b"][:, :], ("combbf", "c_ident_b"), "ptb")
            copy(p, combT[:, t2 * 128:(t2 + 1) * 128], ptb[0:8, :], ("ptb",),
                 "combT", eng="scalar")
        if upto == 1:
            dma(p, y_d[0:128, 0:64], comb[:, :], ("comb",), "yp", eng="gpsimd")
            p.finish()
            return nc
        if layer == 0:
            for dc in range(2):
                for t2g in range(2):
                    for dt in range(8):
                        b = dt % 2
                        dma(p, w2stg[b][:, :],
                            shw_d[dt * 128:(dt + 1) * 128,
                                  dc * 512:(dc + 1) * 512], (f"w2stg{b}",),
                            f"w2stg{b}")
                        copy(p, w2bf[b][:, :], w2stg[b][:, :], (f"w2stg{b}",),
                             f"w2bf{b}", eng="gpsimd")
                        for t2l in range(4):
                            t2 = t2g * 4 + t2l
                            mm(p, pbO[t2l][:, 0:512],
                               xnT[:, dt * 1024 + t2 * 128: dt * 1024 + (t2 + 1) * 128],
                               w2bf[b][:, :], ("xnT", f"w2bf{b}"), f"pbO{t2l}",
                               start=(dt == 0), stop=(dt == 7))
                    for t2l in range(4):
                        t2 = t2g * 4 + t2l
                        bb = t2l % 2
                        vtt(p, tmpS[bb][:, :], pbO[t2l][:, 0:512],
                            shbB[:, dc * 512:(dc + 1) * 512], ALU.add,
                            (f"pbO{t2l}", "shb"), f"tmpS{bb}")
                        act(p, tmpS[bb][:, :], tmpS[bb][:, :], AF.Gelu,
                            (f"tmpS{bb}",), f"tmpS{bb}")
                        vts(p, OUT[:, t2 * 1024 + dc * 512: t2 * 1024 + (dc + 1) * 512],
                            tmpS[bb][:, :], 0.5, ALU.mult, (f"tmpS{bb}",), "OUT")
        else:
            for t2 in range(8):
                memset(p, OUT[:, t2 * 1024:(t2 + 1) * 1024], 0.0, "OUT")
        if upto == 2:
            dma(p, y_d[0:128, :], OUT[:, 0:1024], ("OUT",), "yp", eng="gpsimd")
            p.finish()
            return nc
        for e in range(8):
            dma(p, b1stg[:, :], b1_d[e].rearrange("(a q) -> a q", q=128),
                ("b1stg",), "b1stg")
            transpose(p, ptf[:, 0:32], b1stg[:, :], cs["ident_f"][0:32, 0:32],
                      ("b1stg", "c_ident_f"), "ptf")
            copy(p, b1col[:, :], ptf[:, 0:32], ("ptf",), "b1col", eng="scalar")
            for hin in range(32):
                b = hin % 2
                for r in range(2):
                    src = w1_d[e, r * 512:(r + 1) * 512, hin * 128:(hin + 1) * 128]
                    dma(p, w1stg[b][:, r * 512:(r + 1) * 512].rearrange(
                        "q (dt c) -> q dt c", dt=4),
                        src.rearrange("(dt q) c -> q dt c", q=128),
                        ("w1stg0",), "w1stg0")
                tgt = w1bf[:, :].rearrange("q (dt c) -> q dt c", dt=8)
                copy(p, tgt[:, :, hin * 128:(hin + 1) * 128],
                     w1stg[b][:, :].rearrange("q (dt c) -> q dt c", dt=8),
                     ("w1stg0",), f"w1bf{hin % 4}", eng="vector")
            for ch in range(2):
                for hin in range(32):
                    b = hin % 2
                    for dt in range(8):
                        mm(p, pbH[b][:, 0:512],
                           w1bf[:, dt * 4096 + hin * 128: dt * 4096 + (hin + 1) * 128],
                           xnT[:, dt * 1024 + ch * 512: dt * 1024 + (ch + 1) * 512],
                           (f"w1bf{hin % 4}", "xnT"), f"pbH{b}",
                           start=(dt == 0), stop=(dt == 7))
                    act(p, hbf[:, hin * 512:(hin + 1) * 512], pbH[b][:, 0:512],
                        AFU, (f"pbH{b}", "b1col"), f"hbf{hin % 4}",
                        bias=b1col[:, hin: hin + 1])
                for dh in range(2):
                    for hin in range(32):
                        b = hin % 2
                        dma(p, w2stg[b][:, :],
                            w2_d[e, hin * 128:(hin + 1) * 128,
                                 dh * 512:(dh + 1) * 512],
                            (f"w2stg{b}",), f"w2stg{b}")
                        copy(p, w2bf[b][:, :], w2stg[b][:, :], (f"w2stg{b}",),
                             f"w2bf{b}", eng="gpsimd")
                        for ts4 in range(4):
                            mm(p, pbO[ts4][:, 0:512],
                               hbf[:, hin * 512 + ts4 * 128: hin * 512 + ts4 * 128 + 128],
                               w2bf[b][:, :], (f"hbf{hin % 4}", f"w2bf{b}"),
                               f"pbO{ts4}", start=(hin == 0), stop=(hin == 31))
                    for ts4 in range(4):
                        t2 = ch * 4 + ts4
                        bb = ts4 % 2
                        vts(p, tmpS[bb][:, :], pbO[ts4][:, 0:512],
                            comb[:, t2 * 8 + e: t2 * 8 + e + 1], ALU.mult,
                            (f"pbO{ts4}", "comb"), f"tmpS{bb}")
                        vtt(p, OUT[:, t2 * 1024 + dh * 512: t2 * 1024 + (dh + 1) * 512],
                            OUT[:, t2 * 1024 + dh * 512: t2 * 1024 + (dh + 1) * 512],
                            tmpS[bb][:, :], ALU.add, (f"tmpS{bb}", "OUT"), "OUT")
        if upto == 3:
            dma(p, y_d[0:128, :], OUT[:, 0:1024], ("OUT",), "yp", eng="gpsimd")
            p.finish()
            return nc
        for t2 in range(8):
            tt = hf * 8 + t2
            for dh in range(2):
                mm(p, pbG[:, 0:512], combT[:, t2 * 128:(t2 + 1) * 128],
                   b2bf[:, dh * 512:(dh + 1) * 512], ("combT", "b2bf"), "pbH0")
                vtt(p, OUT[:, t2 * 1024 + dh * 512: t2 * 1024 + (dh + 1) * 512],
                    OUT[:, t2 * 1024 + dh * 512: t2 * 1024 + (dh + 1) * 512],
                    pbG[:, 0:512], ALU.add, ("pbH0", "OUT"), "OUT")
            b = 0
            dma(p, xres[b][:, :], x_d[tt * 128:(tt + 1) * 128, :],
                ("xres0",), "xres0")
            osl = OUT[:, t2 * 1024:(t2 + 1) * 1024]
            if layer == 0:
                emit_ln_full(p, fnrm[:, :], osl, xc, sq, st, "OUT", "nrm",
                             "pln", eb[:, 0:1], "eb", W=D)
                vtt(p, fnrm[:, :], fnrm[:, :], gamB[:, :], ALU.mult,
                    ("nrm", "gam"), "nrm")
                vtt(p, fnrm[:, :], fnrm[:, :], betB[:, :], ALU.add,
                    ("nrm", "bet"), "nrm")
                vtt(p, xres[b][:, :], xres[b][:, :], fnrm[:, :], ALU.add,
                    ("xres0", "nrm"), "xres0")
            else:
                vtt(p, xres[b][:, :], xres[b][:, :], osl, ALU.add,
                    ("xres0", "OUT"), "xres0")
            dma(p, y_d[tt * 128:(tt + 1) * 128, :], xres[b][:, :],
                ("xres0",), "yB", eng="gpsimd")
    p.finish()
    return nc


def stageB_inmaps(x_core, w, layer=0):
    cnames = ["ones_f", "ident_b", "ident_f"]
    m = const_inmaps(cnames)
    pre = "L0_ffn_" if layer == 0 else "L1_ffn_"
    if layer == 0:
        m.update({
            "x": np.ascontiguousarray(x_core),
            "W1": w[pre + "W1"], "b1": w[pre + "b1"],
            "W2": w[pre + "W2"], "b2": w[pre + "b2"],
            "gateW": w[pre + "gate_W"],
            "sharedW": w[pre + "shared_W"],
            "sharedb": np.ascontiguousarray(w[pre + "shared_b"][None, :]),
            "gamma": np.ascontiguousarray(w[pre + "gamma"][None, :]),
            "beta": np.ascontiguousarray(w[pre + "beta"][None, :]),
        })
    else:
        m.update({
            "x": np.ascontiguousarray(x_core),
            "W1": w[pre + "w1"], "b1": w[pre + "b1"],
            "W2": w[pre + "w2"], "b2": w[pre + "b2"],
            "gateW": w[pre + "gate_w"],
            "gateb": np.ascontiguousarray(w[pre + "gate_b"][None, :]),
        })
    return m


def build_stageC(nc):
    x_d = nc.dram_tensor("x", [T, D], F32, kind="ExternalInput").ap()
    nw_d = nc.dram_tensor("normw", [1, D], F32, kind="ExternalInput").ap()
    wq_d = nc.dram_tensor("WqT", [D, D], F32, kind="ExternalInput").ap()
    wk_d = nc.dram_tensor("WkT", [D, D], F32, kind="ExternalInput").ap()
    wv_d = nc.dram_tensor("WvT", [D, D], F32, kind="ExternalInput").ap()
    wo_d = nc.dram_tensor("WoT", [D, D], F32, kind="ExternalInput").ap()
    wu_d = nc.dram_tensor("WuT", [D, D], F32, kind="ExternalInput").ap()
    y_d = nc.dram_tensor("y", [T, D], F32, kind="ExternalOutput").ap()
    cnames = ["ones_f", "ones_b", "ident_b", "L256_b", "L256_f", "rcnt", "scmask"]
    caps = declare_consts(nc, cnames)

    p = Prog(nc)
    p.start()
    cs = load_consts(p, {n: caps[n] for n in cnames}, group="main")

    pA = p.psum("pA", [128, 512], F32)
    pB = p.psum("pB", [128, 512], F32)
    pC = p.psum("pC", [128, 512], F32)
    pD = p.psum("pD", [128, 512], F32)
    pS = p.psum("pS", [128, 512], F32)
    pR = p.psum("pR", [128, 512], F32)
    ptb = p.psum("ptb", [128, 128], BF16)

    SEG, N, H = 256, 8, 16

    preWoT = p.sbuf("preWoT", [128, 16384], BF16)
    bs = p.sbuf("bs", [64, 8192], BF16)
    msT = p.sbuf("msT", [128, 64], BF16)
    sc = p.sbuf("sc", [128, 144], F32)
    eb = p.sbuf("eb", [128, 2], F32)
    nwB = p.sbuf("nwB", [128, 1024], F32)
    memset(p, eb[:, 0:1], 1e-5 * D, "eb")
    memset(p, eb[:, 1:2], 1e-6 * D, "eb2")

    gs = "scr"
    xt0 = p.sbuf("xt0", [128, 1024], F32, group=gs)
    xLf = p.sbuf("xLf", [128, 1024], F32, group=gs)
    xnf = p.sbuf("xnf", [128, 1024], F32, group=gs)
    xnb = p.sbuf("xnb", [128, 1024], BF16, group=gs)
    xlb = p.sbuf("xlb", [128, 1024], BF16, group=gs)
    xc = p.sbuf("xc", [128, 1024], F32, group=gs)
    sq = p.sbuf("sq", [128, 1024], F32, group=gs)
    st = p.sbuf("st", [128, 16], F32, group=gs)
    wstg = p.sbuf("wstg", [128, 1024], F32, group=gs)

    g1 = "p1"
    xLT = p.sbuf("xLT", [128, 16384], BF16, group=g1)
    mcT = p.sbuf("mcT", [128, 16384], BF16, group=g1)
    mtok = p.sbuf("mtok", [128, 256], BF16, group=g1)

    dma(p, wstg[0:1, 0:1024], nw_d[:, :], ("wstg",), "wstg")
    for dc in range(2):
        mm(p, pA[:, 0:512], cs["ones_f"][0:1, :],
           wstg[0:1, dc * 512:(dc + 1) * 512], ("wstg", "c_ones_f"), "pA")
        copy(p, nwB[:, dc * 512:(dc + 1) * 512], pA[:, 0:512], ("pA",), "nwB",
             eng="scalar")
    for tt in range(16):
        dma(p, xt0[:, :], x_d[tt * 128:(tt + 1) * 128, :], ("xt0",), "xt0")
        emit_ln_full(p, xLf[:, :], xt0[:, :], xc, sq, st, "xt0", "xLf", "ln",
                     eb[:, 0:1], "eb", W=D)
        copy(p, xlb[:, :], xLf[:, :], ("xLf",), "xlb", eng="scalar")
        for dt in range(8):
            transpose(p, ptb[:, :], xlb[:, dt * 128:(dt + 1) * 128],
                      cs["ident_b"][:, :], ("xlb", "c_ident_b"), "ptb")
            copy(p, xLT[:, dt * 2048 + tt * 128: dt * 2048 + (tt + 1) * 128],
                 ptb[:, :], ("ptb",), "xLT", eng="vector")
    for dt in range(8):
        for n in range(N):
            rsum(p, st[:, 8:9], xLT[:, dt * 2048 + n * 256: dt * 2048 + (n + 1) * 256],
                 ("xLT",), "msum")
            vts(p, msT[:, dt * 8 + n: dt * 8 + n + 1], st[:, 8:9], 1.0 / SEG,
                ALU.mult, ("msum",), "msT")
    for n in range(N):
        for dt in range(8):
            for r in range(2):
                transpose(p, ptb[:, :],
                          xLT[:, dt * 2048 + (2 * n + r) * 128:
                              dt * 2048 + (2 * n + r + 1) * 128],
                          cs["ident_b"][:, :], ("xLT", "c_ident_b"), "ptb")
                copy(p, mtok[:, r * 128:(r + 1) * 128], ptb[:, :], ("ptb",),
                     "mtok", eng="scalar")
            for r in range(2):
                mm(p, pB[:, 0:256], mtok[:, r * 128:(r + 1) * 128],
                   cs["L256_b"][:, r * 256:(r + 1) * 256], ("mtok", "c_L256_b"),
                   "pB", start=(r == 0), stop=(r == 1))
            vtt(p, mcT[:, dt * 2048 + n * 256: dt * 2048 + (n + 1) * 256],
                pB[:, 0:256], cs["rcnt"][:, :], ALU.mult, ("pB", "c_rcnt"), "mcT")

    g2 = "p2"
    u_dt = p.sbuf("u_dt", [128, 2048], BF16, group=g2)
    wdt = p.sbuf("wdt", [128, 1024], BF16, group=g2)
    tmpu = [p.sbuf(f"tmpu{i}", [128, 512], F32, group=g2) for i in range(2)]
    scr = p.sbuf("scr", [1, 2048], F32, group=g2)
    scT_f = p.sbuf("scT_f", [8, 2048], F32, group=g2)
    hi_f = p.sbuf("hi_f", [8, 2048], F32, group=g2)
    scT = p.sbuf("scT", [8, 2048], BF16, group=g2)
    scTlo = p.sbuf("scTlo", [8, 2048], BF16, group=g2)
    trh = p.sbuf("trh", [128, 8], F32, group=g2)
    trl = p.sbuf("trl", [128, 8], F32, group=g2)
    qbank = [(pA, "pA"), (pB, "pB"), (pC, "pC"), (pD, "pD")]
    for dt in range(8):
        for di in range(8):
            dma(p, wstg[:, di * 128:(di + 1) * 128],
                wu_d[di * 128:(di + 1) * 128, dt * 128:(dt + 1) * 128],
                ("wstg",), "wstg")
        copy(p, wdt[:, :], wstg[:, :], ("wstg",), "wdt", eng="vector")
        for qc in range(4):
            for di in range(8):
                mm(p, pR[:, 0:512], wdt[:, di * 128:(di + 1) * 128],
                   xLT[:, di * 2048 + qc * 512: di * 2048 + (qc + 1) * 512],
                   ("wdt", "xLT"), "pR", start=(di == 0), stop=(di == 7))
            copy(p, u_dt[:, qc * 512:(qc + 1) * 512], pR[:, 0:512], ("pR",),
                 "u_dt", eng="scalar")
        for qc in range(4):
            mm(p, pS[0:8, 0:512] if False else pS[0:8, 0:512],
               msT[:, dt * 8:(dt + 1) * 8],
               u_dt[:, qc * 512:(qc + 1) * 512], ("msT", "u_dt"),
               f"pS{qc}", start=(dt == 0), stop=(dt == 7)) if False else None
        for qc in range(4):
            bank, bk = qbank[qc]
            mm(p, bank[0:8, 0:512], msT[:, dt * 8:(dt + 1) * 8],
               u_dt[:, qc * 512:(qc + 1) * 512], ("msT", "u_dt"), bk,
               start=(dt == 0), stop=(dt == 7))
        for qc in range(4):
            b = qc % 2
            vtt(p, tmpu[b][:, :], u_dt[:, qc * 512:(qc + 1) * 512],
                mcT[:, dt * 2048 + qc * 512: dt * 2048 + (qc + 1) * 512],
                ALU.mult, ("u_dt", "mcT"), f"tmpu{b}")
            mm(p, pS[0:1, 0:512], cs["ones_f"][:, 0:1], tmpu[b][:, :],
               (f"tmpu{b}", "c_ones_f"), "pScur")
            if dt == 0:
                vts(p, scr[0:1, qc * 512:(qc + 1) * 512], pS[0:1, 0:512], 1.0,
                    ALU.mult, ("pScur",), "scr")
            else:
                vtt(p, scr[0:1, qc * 512:(qc + 1) * 512],
                    scr[0:1, qc * 512:(qc + 1) * 512], pS[0:1, 0:512],
                    ALU.add, ("pScur", "scr"), "scr")
    for qc in range(4):
        bank, bk = qbank[qc]
        copy(p, scT_f[:, qc * 512:(qc + 1) * 512], bank[0:8, 0:512], (bk,),
             "scT_f", eng="scalar")
    copy(p, scT[:, :], scT_f[:, :], ("scT_f",), "scT", eng="vector")
    copy(p, hi_f[:, :], scT[:, :], ("scT",), "hi_f", eng="vector")
    vtt(p, hi_f[:, :], scT_f[:, :], hi_f[:, :], ALU.subtract,
        ("scT_f", "hi_f"), "lo_f")
    copy(p, scTlo[:, :], hi_f[:, :], ("lo_f",), "scTlo", eng="vector")
    for tt in range(16):
        transpose(p, ptb[:, 0:8], scT[:, tt * 128:(tt + 1) * 128],
                  cs["ident_b"][0:8, 0:8], ("scT", "c_ident_b"), "ptb")
        copy(p, trh[:, :], ptb[:, 0:8], ("ptb",), "trh", eng="scalar")
        transpose(p, ptb[:, 0:8], scTlo[:, tt * 128:(tt + 1) * 128],
                  cs["ident_b"][0:8, 0:8], ("scTlo", "c_ident_b", "trh"), "ptb")
        copy(p, trl[:, :], ptb[:, 0:8], ("ptb",), "trl", eng="scalar")
        vtt(p, sq[:, 0:8], trh[:, :], trl[:, :], ALU.add, ("trh", "trl"), "scc")
        vts(p, sq[:, 0:8], sq[:, 0:8], 1.0 / 32.0, ALU.mult, ("scc",), "scc")
        vtt(p, sc[:, 9 * tt: 9 * tt + 8], sq[:, 0:8],
            cs["scmask"][:, 8 * tt: 8 * tt + 8], ALU.add, ("scc", "c_scmask"), "sc")
        mm(p, pR[:, 0:1], scr[0:1, tt * 128:(tt + 1) * 128],
           cs["ones_f"][0:1, 0:1], ("scr", "c_ones_f"), "pR")
        vts(p, sc[:, 9 * tt + 8: 9 * tt + 9], pR[:, 0:1], 1.0 / 32.0,
            ALU.mult, ("pR", "sc"), "sc")
    for tt in range(16):
        rmax(p, st[:, 0:1], sc[:, 9 * tt: 9 * (tt + 1)], ("sc",), "gm")
        vts(p, st[:, 1:2], st[:, 0:1], -1.0, ALU.mult, ("gm",), "gm")
        act(p, sc[:, 9 * tt: 9 * (tt + 1)], sc[:, 9 * tt: 9 * (tt + 1)], AF.Exp,
            ("sc", "gm"), "sc", bias=st[:, 1:2])
        rsum(p, st[:, 2:3], sc[:, 9 * tt: 9 * (tt + 1)], ("sc",), "gs")
        recip(p, st[:, 3:4], st[:, 2:3], ("gs",), "gs")
        vts(p, sc[:, 9 * tt: 9 * (tt + 1)], sc[:, 9 * tt: 9 * (tt + 1)],
            st[:, 3:4], ALU.mult, ("sc", "gs"), "sc")
    p.free_group(g2)
    p.free_group(g1)

    pqT = p.sbuf("pqT", [128, 16384], BF16)
    g34 = "p34"
    pkT = p.sbuf("pkT", [128, 16384], BF16, group=g34)
    vT = p.sbuf("vT", [128, 16384], BF16, group=g34)
    g3 = "p3"
    xnT = p.sbuf("xnT", [128, 8192], BF16, group=g3)
    wdt2 = p.sbuf("wdt2", [128, 1024], BF16, group=g3)
    e1 = p.sbuf("e1", [128, 512], F32, group=g3)
    e2 = p.sbuf("e2", [128, 512], F32, group=g3)
    e3 = p.sbuf("e3", [128, 512], F32, group=g3)
    for th2 in range(2):
        for t2 in range(8):
            tt = th2 * 8 + t2
            dma(p, xt0[:, :], x_d[tt * 128:(tt + 1) * 128, :], ("xt0",), "xt0")
            emit_ln_full(p, xLf[:, :], xt0[:, :], xc, sq, st, "xt0", "xLf", "ln3",
                         eb[:, 0:1], "eb", W=D)
            emit_rms(p, xnf[:, :], xLf[:, :], sq, st, "xLf", "xnf0", "rms",
                     eb[:, 1:2], "eb2", W=D)
            vtt(p, xnf[:, :], xnf[:, :], nwB[:, :], ALU.mult, ("xnf0", "nwB"), "xnf")
            copy(p, xnb[:, :], xnf[:, :], ("xnf",), "xnb", eng="scalar")
            for dt in range(8):
                transpose(p, ptb[:, :], xnb[:, dt * 128:(dt + 1) * 128],
                          cs["ident_b"][:, :], ("xnb", "c_ident_b"), "ptb")
                copy(p, xnT[:, dt * 1024 + t2 * 128: dt * 1024 + (t2 + 1) * 128],
                     ptb[:, :], ("ptb",), "xnT", eng="vector")
        for w_ap, outT, okey, kind in ((wq_d, pqT, "pqT", "elu"),
                                       (wk_d, pkT, "pkT", "elu"),
                                       (wv_d, vT, "vT", "copy")):
            for dt in range(8):
                for di in range(8):
                    dma(p, wstg[:, di * 128:(di + 1) * 128],
                        w_ap[di * 128:(di + 1) * 128, dt * 128:(dt + 1) * 128],
                        ("wstg",), "wstg")
                copy(p, wdt2[:, :], wstg[:, :], ("wstg",), "wdt2", eng="vector")
                for qcl in range(2):
                    qc = th2 * 2 + qcl
                    for di in range(8):
                        mm(p, pA[:, 0:512], wdt2[:, di * 128:(di + 1) * 128],
                           xnT[:, di * 1024 + qcl * 512: di * 1024 + (qcl + 1) * 512],
                           ("wdt2", "xnT"), "pA", start=(di == 0), stop=(di == 7))
                    sl = outT[:, dt * 2048 + qc * 512: dt * 2048 + (qc + 1) * 512]
                    if kind == "copy":
                        copy(p, sl, pA[:, 0:512], ("pA",), okey, eng="scalar")
                    else:
                        vts(p, e1[:, :], pA[:, 0:512], 0.0, ALU.min, ("pA",), "e1")
                        act(p, e2[:, :], e1[:, :], AF.Exp, ("e1",), "e2")
                        vts(p, e3[:, :], pA[:, 0:512], 0.0, ALU.max,
                            ("pA", "e2"), "e3")
                        vtt(p, sl, e2[:, :], e3[:, :], ALU.add, ("e2", "e3"), okey)
    p.free_group(g3)

    g4 = "p4"
    pq_s = p.sbuf("pq_s", [64, 256], BF16, group=g4)
    pk_s = p.sbuf("pk_s", [64, 256], BF16, group=g4)
    v_s = p.sbuf("v_s", [64, 256], BF16, group=g4)
    pk_tok = p.sbuf("pk_tok", [128, 128], BF16, group=g4)
    v_tok = p.sbuf("v_tok", [128, 128], BF16, group=g4)
    ATbf = p.sbuf("ATbf", [128, 512], BF16, group=g4)
    zfull = p.sbuf("zfull", [64, 256], F32, group=g4)
    pqf = p.sbuf("pqf", [64, 256], F32, group=g4)
    tmpD = p.sbuf("tmpD", [64, 256], F32, group=g4)
    rd = p.sbuf("rd", [1, 256], F32, group=g4)
    rdB = p.sbuf("rdB", [64, 256], F32, group=g4)
    Sx = p.sbuf("Sx", [64, 64], BF16, group=g4)
    Sex = p.sbuf("Sex", [64, 1024], F32, group=g4)
    zex = p.sbuf("zex", [64, 16], F32, group=g4)
    zrow = p.sbuf("zrow", [64, 1], F32, group=g4)
    d2 = p.sbuf("d2", [1, 256], F32, group=g4)
    memset(p, Sex[:, :], 0.0, "Sex")
    memset(p, zex[:, :], 0.0, "zex")
    for h in range(H):
        th, hb = h // 2, (h % 2) * 64
        for n in range(N):
            colq = th * 2048 + n * 256
            copy(p, pq_s[:, :], pqT[hb:hb + 64, colq:colq + 256], ("pqT",),
                 "pq_s", eng="vector")
            copy(p, pk_s[:, :], pkT[hb:hb + 64, colq:colq + 256], ("pkT",),
                 "pk_s", eng="vector")
            copy(p, v_s[:, :], vT[hb:hb + 64, colq:colq + 256], ("vT",),
                 "v_s", eng="vector")
            for r in range(2):
                transpose(p, ptb[:, 0:64], pk_s[:, r * 128:(r + 1) * 128],
                          cs["ident_b"][0:64, 0:64], ("pk_s", "c_ident_b"), "ptb")
                copy(p, pk_tok[:, r * 64:(r + 1) * 64], ptb[:, 0:64], ("ptb",),
                     "pk_tok", eng="scalar")
                transpose(p, ptb[:, 0:64], v_s[:, r * 128:(r + 1) * 128],
                          cs["ident_b"][0:64, 0:64], ("v_s", "c_ident_b", "pk_tok"),
                          "ptb")
                copy(p, v_tok[:, r * 64:(r + 1) * 64], ptb[:, 0:64], ("ptb",),
                     "v_tok", eng="scalar")
            for r in range(2):
                mm(p, pS[0:64, 0:64], pk_tok[:, r * 64:(r + 1) * 64],
                   v_tok[:, r * 64:(r + 1) * 64], ("pk_tok", "v_tok"), "pS",
                   start=(r == 0), stop=(r == 1))
            for r, bank, bk in ((0, pA, "pA"), (1, pB, "pB")):
                mm(p, bank[:, 0:256], pk_s[:, r * 128:(r + 1) * 128],
                   pq_s[:, :], ("pk_s", "pq_s"), bk)
                vtt(p, ATbf[:, r * 256:(r + 1) * 256], bank[:, 0:256],
                    cs["L256_f"][:, r * 256:(r + 1) * 256], ALU.mult,
                    (bk, "c_L256_f"), "ATbf")
            for r in range(2):
                mm(p, pC[0:64, 0:256], pk_tok[:, r * 64:(r + 1) * 64],
                   cs["L256_b"][:, r * 256:(r + 1) * 256], ("pk_tok", "c_L256_b"),
                   "pC", start=(r == 0), stop=(r == 1))
            vts(p, zfull[:, :], pC[0:64, 0:256], zex[:, h:h + 1], ALU.add,
                ("pC", "zex"), "zfull")
            copy(p, pqf[:, :], pq_s[:, :], ("pq_s",), "pqf", eng="vector")
            vtt(p, tmpD[:, :], pqf[:, :], zfull[:, :], ALU.mult,
                ("pqf", "zfull"), "tmpD")
            mm(p, pD[0:1, 0:256], cs["ones_f"][0:64, 0:1], tmpD[:, :],
               ("tmpD", "c_ones_f"), "pD")
            vts(p, d2[0:1, :], pD[0:1, 0:256], 1e-6, ALU.add, ("pD",), "d2")
            recip(p, rd[0:1, :], d2[0:1, :], ("d2",), "rd")
            mm(p, pD[0:64, 0:256], cs["ones_f"][0:1, 0:64], rd[0:1, 0:256],
               ("rd", "c_ones_f"), "pD")
            copy(p, rdB[:, :], pD[0:64, 0:256], ("pD",), "rdB", eng="scalar")
            copy(p, Sx[:, :], Sex[:, h * 64:(h + 1) * 64], ("Sex",), "Sx",
                 eng="vector")
            mm(p, pB[0:64, 0:256], Sx[:, :], pq_s[:, :], ("Sx", "pq_s", "ATbf"),
               "pB", start=True, stop=False)
            for r in range(2):
                mm(p, pB[0:64, 0:256], v_tok[:, r * 64:(r + 1) * 64],
                   ATbf[:, r * 256:(r + 1) * 256], ("v_tok", "ATbf"), "pB",
                   start=False, stop=(r == 1))
            vtt(p, preWoT[hb:hb + 64, colq:colq + 256], pB[0:64, 0:256],
                rdB[:, :], ALU.mult, ("pB", "rdB"), "preWoT")
            vtt(p, Sex[:, h * 64:(h + 1) * 64], Sex[:, h * 64:(h + 1) * 64],
                pS[0:64, 0:64], ALU.add, ("pS", "Sex", "Sx"), "Sex")
            copy(p, bs[:, (h * 8 + n) * 64:(h * 8 + n + 1) * 64],
                 Sex[:, h * 64:(h + 1) * 64], ("Sex",), "bs", eng="scalar")
            rsum(p, zrow[:, :], pk_s[:, :], ("pk_s", "zfull"), "zrow")
            vtt(p, zex[:, h:h + 1], zex[:, h:h + 1], zrow[:, :], ALU.add,
                ("zex", "zrow"), "zex")
    p.free_group(g4)
    p.free_group(g34)

    g5 = "p5"
    hacc = p.sbuf("hacc", [128, 16384], F32, group=g5)
    hstg = p.sbuf("hstg", [128, 512], F32, group=g5)
    pq2 = p.sbuf("pq2", [64, 128], BF16, group=g5)
    t1 = p.sbuf("t1", [128, 64], F32, group=g5)
    for h in range(H):
        th, hb = h // 2, (h % 2) * 64
        for tt in range(16):
            copy(p, pq2[:, :], pqT[hb:hb + 64, th * 2048 + tt * 128:
                                   th * 2048 + (tt + 1) * 128], ("pqT",),
                 "pq2", eng="vector")
            hsl = hacc[:, tt * 1024 + h * 64: tt * 1024 + (h + 1) * 64]
            for n in range(N):
                mm(p, pR[:, 0:64], pq2[:, :],
                   bs[:, (h * 8 + n) * 64:(h * 8 + n + 1) * 64], ("pq2", "bs"),
                   "pR")
                if n == 0:
                    vts(p, hsl, pR[:, 0:64], sc[:, 9 * tt + n: 9 * tt + n + 1],
                        ALU.mult, ("pR", "sc"), "hacc")
                else:
                    vts(p, t1[:, :], pR[:, 0:64], sc[:, 9 * tt + n: 9 * tt + n + 1],
                        ALU.mult, ("pR", "sc"), "t1")
                    vtt(p, hsl, hsl, t1[:, :], ALU.add, ("t1", "hacc"), "hacc")
    g6 = "p6"
    woB = p.sbuf("woB", [128, 8192], BF16, group=g6)
    t2b = [p.sbuf(f"t2b{i}", [128, 512], F32, group=g6) for i in range(2)]
    _load_weight_tiles(p, woB, wo_d, D, D, wstg, "woB")
    for tt in range(16):
        dma(p, xt0[:, :], x_d[tt * 128:(tt + 1) * 128, :], ("xt0",), "xt0")
        emit_ln_full(p, xLf[:, :], xt0[:, :], xc, sq, st, "xt0", "xLf", "ln6",
                     eb[:, 0:1], "eb", W=D)
        for dh in range(2):
            bk = ("pA", pA) if dh == 0 else ("pB", pB)
            for dt in range(8):
                mm(p, bk[1][:, 0:512],
                   preWoT[:, dt * 2048 + tt * 128: dt * 2048 + (tt + 1) * 128],
                   woB[:, dt * 1024 + dh * 512: dt * 1024 + (dh + 1) * 512],
                   ("preWoT", "woB"), bk[0], start=(dt == 0), stop=(dt == 7))
            b = dh
            vtt(p, t2b[b][:, :], xLf[:, dh * 512:(dh + 1) * 512], bk[1][:, 0:512],
                ALU.add, ("xLf", bk[0]), f"t2b{b}")
            vts(p, t2b[b][:, :], t2b[b][:, :], sc[:, 9 * tt + 8: 9 * tt + 9],
                ALU.mult, (f"t2b{b}", "sc"), f"t2b{b}")
            vtt(p, t2b[b][:, :], t2b[b][:, :],
                hacc[:, tt * 1024 + dh * 512: tt * 1024 + (dh + 1) * 512],
                ALU.add, (f"t2b{b}", "hacc"), f"t2b{b}")
            vtt(p, xt0[:, dh * 512:(dh + 1) * 512],
                xt0[:, dh * 512:(dh + 1) * 512], t2b[b][:, :], ALU.add,
                ("xt0", f"t2b{b}", "xLf"), "xt0")
        dma(p, y_d[tt * 128:(tt + 1) * 128, :], xt0[:, :], ("xt0",), "yC",
            eng="gpsimd")
    p.finish()
    return nc


def stageC_inmaps(x_core, w):
    cnames = ["ones_f", "ones_b", "ident_b", "L256_b", "L256_f", "rcnt", "scmask"]
    m = const_inmaps(cnames)
    m.update({
        "x": np.ascontiguousarray(x_core),
        "normw": np.ascontiguousarray(w["L1_attn_norm_w"][None, :]),
        "WqT": np.ascontiguousarray(w["L1_attn_Wq"].T),
        "WkT": np.ascontiguousarray(w["L1_attn_Wk"].T),
        "WvT": np.ascontiguousarray(w["L1_attn_Wv"].T),
        "WoT": np.ascontiguousarray(w["L1_attn_Wo"].T),
        "WuT": np.ascontiguousarray(w["L1_attn_Wu"].T),
    })
    return m


_NC_CACHE = {}


def _get_nc(key, builder):
    if key not in _NC_CACHE:
        _NC_CACHE[key] = builder(bass.Bass("TRN2", target_bir_lowering=False))
    return _NC_CACHE[key]


def kernel(**inputs):
    w = {k: np.ascontiguousarray(np.asarray(v, np.float32)) for k, v in inputs.items()}
    x = w["L0_attn_x"]

    nc = _get_nc("A", build_stageA)
    res = run_bass_kernel_spmd(nc, [stageA_inmaps(x[b], w) for b in range(NCORES)],
                               core_ids=list(range(NCORES)))
    x1 = [res.results[b]["y"] for b in range(NCORES)]

    nc = _get_nc("B0", lambda n: build_stageB(n, layer=0))
    res = run_bass_kernel_spmd(nc, [stageB_inmaps(x1[b], w, layer=0)
                                    for b in range(NCORES)],
                               core_ids=list(range(NCORES)))
    x2 = [res.results[b]["y"] for b in range(NCORES)]

    nc = _get_nc("C", build_stageC)
    res = run_bass_kernel_spmd(nc, [stageC_inmaps(x2[b], w) for b in range(NCORES)],
                               core_ids=list(range(NCORES)))
    x3 = [res.results[b]["y"] for b in range(NCORES)]

    nc = _get_nc("B1", lambda n: build_stageB(n, layer=1))
    res = run_bass_kernel_spmd(nc, [stageB_inmaps(x3[b], w, layer=1)
                                    for b in range(NCORES)],
                               core_ids=list(range(NCORES)))
    out = np.stack([np.asarray(res.results[b]["y"], np.float32)
                    for b in range(NCORES)], axis=0)
    return out
